# version 11 (final)
# baseline (speedup 1.0000x reference)
import numpy as np
import ml_dtypes
import concourse.bass as bass
import concourse.mybir as mybir
from concourse.bass_utils import run_bass_kernel_spmd

F32 = mybir.dt.float32
BF16 = mybir.dt.bfloat16
U8 = mybir.dt.uint8
AF = mybir.ActivationFunctionType
ALU = mybir.AluOpType
AX = mybir.AxisListType
BF = ml_dtypes.bfloat16

MAGIC = float(np.float32(1.5 * 2.0 ** 23))
LN_EPS = 1e-5
DEN_EPS = 1e-8
QEPS = 1e-5

NT = 16
PHASE_LIMIT = [99]
DBG = [0]
A_LIM = [99]
D = 1024
NH = 16
INNER = 4096
KB = 1024


class Sem:
    def __init__(self, handle):
        self.h = handle
        self.n = 0

    def inc(self, instr, k=1):
        instr.then_inc(self.h, k)
        self.n += k
        return instr


def build_nc():
    nc = bass.Bass("TRN2", target_bir_lowering=False,
                   detect_race_conditions=False)

    x_in = nc.dram_tensor("x", [NT * 128, D], F32, kind="ExternalInput")
    wq_d = nc.dram_tensor("wq", [D, D], F32, kind="ExternalInput")
    wk_d = nc.dram_tensor("wk", [D, D], F32, kind="ExternalInput")
    wv_d = nc.dram_tensor("wv", [D, D], F32, kind="ExternalInput")
    wfc_d = nc.dram_tensor("wfc", [D, D], F32, kind="ExternalInput")
    gam_d = nc.dram_tensor("gamma", [1, D], F32, kind="ExternalInput")
    bet_d = nc.dram_tensor("beta", [1, D], F32, kind="ExternalInput")
    wg_d = nc.dram_tensor("wg", [INNER, D], F32, kind="ExternalInput")
    wu_d = nc.dram_tensor("wu", [INNER, D], F32, kind="ExternalInput")
    wd_d = nc.dram_tensor("wd", [D, INNER], F32, kind="ExternalInput")
    y_out = nc.dram_tensor("y", [NT * 128, D], F32, kind="ExternalOutput")

    x1s = nc.dram_tensor("x1s", [NT * 128, D], F32,
                         kind="ExternalOutput" if DBG[0] else "Internal")
    if DBG[0]:
        dbgkv = nc.dram_tensor("dbgkv", [64, NH * 64 + NH], F32, kind="ExternalOutput")
        dbgmv = nc.dram_tensor("dbgmv", [128, 64], F32, kind="ExternalOutput")
        dbgb1 = nc.dram_tensor("dbgb1", [128, 4096], F32, kind="ExternalOutput")
        dbgk2 = nc.dram_tensor("dbgk2", [128, 8 * 130], F32, kind="ExternalOutput")
        dbgnd = nc.dram_tensor("dbgnd", [128, 8 * 130], F32, kind="ExternalOutput")
        dbgb2 = nc.dram_tensor("dbgb2", [128, 4096], F32, kind="ExternalOutput")
        dbgb3 = nc.dram_tensor("dbgb3", [128, 1024], F32, kind="ExternalOutput")
        dbgwfc = nc.dram_tensor("dbgwfc", [128, 8192], BF16, kind="ExternalOutput")
    wqg = nc.dram_tensor("wqg", [INNER, D], BF16)
    wqu = nc.dram_tensor("wqu", [INNER, D], BF16)
    wqd = nc.dram_tensor("wqd", [D, INNER], BF16)
    cc_in = nc.dram_tensor("cc_in", [64, NH * 64 + NH], F32)
    cc_out = nc.dram_tensor("cc_out", [64, NH * 64 + NH], F32)
    sc = nc.dram_tensor("sc", [1, 8], F32)

    from contextlib import ExitStack
    ctx = ExitStack()
    TOT = 206 * KB
    arena = ctx.enter_context(nc.sbuf_tensor("arena", [128, TOT], U8))
    base_addr = nc.lookup_mloc(arena).addr

    def sbt(name, shape, dtype, off):
        per_part = int(np.prod(shape[1:])) * mybir.dt.size(dtype)
        assert off % 32 == 0, name
        assert off + per_part <= TOT, (name, off, per_part)
        return nc.alloc_sbuf_tensor_at(name, list(shape), dtype,
                                       offset=base_addr + off)

    idf = sbt("idf", [128, 128], F32, 0)
    idb = sbt("idb", [128, 128], BF16, 512)
    onec = sbt("onec", [128, 1], BF16, 768)
    onecF = sbt("onecF", [128, 1], F32, 17600)
    G_t = sbt("G_t", [128, D], F32, 1 * KB)
    B_t = sbt("B_t", [128, D], F32, 5 * KB)
    KV2 = sbt("KV2", [128, 8 * 130], BF16, 9 * KB)
    KVs = sbt("KVs", [64, NH * 64 + NH], F32, 11776)
    MV = sbt("MV", [128, 32], F32, 16000)
    SD = sbt("SD", [128, 16], F32, 16128)
    RS = sbt("RS", [128, 16], F32, 16192)
    NMR = sbt("NMR", [128, 16], F32, 16256)
    MMB = sbt("MMB", [128, 16], F32, 16320)
    SVB = sbt("SVB", [128, 16], F32, 16384)
    CG = sbt("CG", [128, 16], F32, 16448)
    CU = sbt("CU", [128, 16], F32, 16512)
    SH = sbt("SH", [128, 16], F32, 16576)
    CD = sbt("CD", [128, 16], F32, 16640)
    MH8 = sbt("MH8", [128, 64], F32, 16704)
    cmg127 = sbt("cmg127", [128, 1], F32, 16960)
    cmu127 = sbt("cmu127", [128, 1], F32, 16992)
    cmd127 = sbt("cmd127", [128, 1], F32, 17024)
    swv = sbt("swv", [128, 1], F32, 17056)
    st1 = sbt("st1", [128, 12], F32, 17088)
    MVst = sbt("MVst", [128, 12], F32, 17152)
    sm1 = sbt("sm1", [128, 2], F32, 17216)
    sm2 = sbt("sm2", [128, 4], F32, 17248)
    d16 = sbt("d16", [128, 16], F32, 17280)
    r16 = sbt("r16", [128, 16], F32, 17344)
    mb1 = sbt("mb1", [128, 1], F32, 17408)
    rb1 = sbt("rb1", [128, 1], F32, 17440)
    scp = sbt("scp", [1, 8], F32, 17472)
    epsv = sbt("epsv", [128, 1], F32, 17504)
    vspc = sbt("vspc", [128, 8], F32, 17536)
    sspc = sbt("sspc", [128, 8], F32, 17568)

    S_W = 18 * KB
    Wq_s = sbt("Wq_s", [128, 8 * D], BF16, S_W)
    Wk_s = sbt("Wk_s", [128, 8 * D], BF16, S_W + 16 * KB)
    Wv_s = sbt("Wv_s", [128, 8 * D], BF16, S_W + 32 * KB)
    Wfc_s = sbt("Wfc_s", [128, 8 * D], BF16, S_W + 48 * KB)
    H_s = sbt("H_s", [128, 8 * INNER], BF16, S_W)
    WdT = sbt("WdT", [128, 32 * KB], BF16, S_W)

    A_T = 82 * KB
    aT = sbt("aT", [128, 16 * KB], BF16, A_T)
    wbuf0 = sbt("wbuf0", [128, 4096], F32, A_T)
    wbuf1 = sbt("wbuf1", [128, 4096], F32, A_T + 16 * KB)
    wgt0 = sbt("wgt0", [128, 4096], BF16, A_T)
    wut0 = sbt("wut0", [128, 4096], BF16, A_T + 8 * KB)
    wgt1 = sbt("wgt1", [128, 4096], BF16, A_T + 16 * KB)
    wut1 = sbt("wut1", [128, 4096], BF16, A_T + 24 * KB)
    HQT = sbt("HQT", [128, 32 * KB], BF16, A_T)

    B_S = 114 * KB
    x0r = [sbt(f"x0r{i}", [128, D], F32, B_S + i * 4 * KB) for i in range(4)]
    af0 = sbt("af0", [128, D], F32, B_S + 16 * KB)
    af1 = sbt("af1", [128, D], F32, B_S + 20 * KB)
    ytb = sbt("ytb", [128, D], F32, B_S + 24 * KB)
    t1b = sbt("t1b", [128, D], F32, B_S + 28 * KB)
    wabs0 = sbt("wabs0", [128, 4096], BF16, B_S + 16 * KB)
    wabs1 = sbt("wabs1", [128, 4096], BF16, B_S + 24 * KB)
    gsb0 = sbt("gsb0", [128, 512], F32, B_S + 16 * KB)
    gsb1 = sbt("gsb1", [128, 512], F32, B_S + 18 * KB)

    C_S = 146 * KB
    Kb = [sbt(f"Kb{i}", [128, D], BF16, C_S + i * 2 * KB) for i in range(2)]
    Vb = [sbt(f"Vb{i}", [128, D], BF16, C_S + 4 * KB + i * 2 * KB) for i in range(2)]
    tern0 = sbt("tern0", [128, 4096], BF16, C_S + 8 * KB)
    tern1 = sbt("tern1", [128, 4096], BF16, C_S + 16 * KB)
    btb = sbt("btb", [128, D], F32, C_S)
    ndb = [sbt("ndbA", [128, 8 * 130], F32, C_S + 4 * KB),
           sbt("ndbB", [128, 8 * 130], F32, C_S + 12 * KB)]
    atn = [sbt(f"atn{i}", [128, D], BF16, C_S + 20 * KB + i * 2 * KB) for i in range(2)]
    atnT = [sbt(f"atnT{i}", [128, D], BF16, C_S + 24 * KB + i * 2 * KB) for i in range(2)]
    QTt = [sbt(f"QTt{i}", [128, D], BF16, S_W + 16 * KB + i * 2 * KB) for i in range(2)]
    bqb = [sbt(f"bqb{i}", [128, D], BF16, S_W + 20 * KB + i * 2 * KB) for i in range(2)]
    HQb = [sbt(f"HQb{i}", [128, 4096], BF16, C_S + i * 8 * KB) for i in range(2)]
    hq1s = sbt("hq1s", [128, 512], F32, C_S + 16 * KB)
    x1t = [sbt(f"x1t{i}", [128, D], F32, C_S + i * 4 * KB) for i in range(2)]
    ftb = [sbt(f"ftb{i}", [128, D], F32, C_S + 8 * KB + i * 4 * KB) for i in range(2)]
    t4b = sbt("t4b", [128, D], F32, C_S + 16 * KB)
    outb = [sbt(f"outb{i}", [128, D], F32, C_S + 20 * KB + i * 4 * KB) for i in range(2)]

    BQT = sbt("BQT", [128, 16 * KB], BF16, 174 * KB)

    PS = [ctx.enter_context(nc.psum_tensor(f"ps{i}", [128, 512], F32))
          for i in range(8)]
    PSB = [p[:, :].bitcast(BF16) for p in PS]

    def mksem(name):
        return Sem(ctx.enter_context(nc.semaphore(name)))

    BAR = mksem("bar")
    gi = mksem("gi")
    d1 = mksem("d1")
    d3 = mksem("d3")
    dOUT = mksem("dOUT")
    dG = mksem("dG")
    cs = mksem("cs")
    sv = {k: mksem("sv_" + k) for k in
          ["stat", "evT", "evV", "evKV", "ln", "q", "T3", "nd", "y", "h", "hq",
           "t3c", "x2", "pw"]}
    ss = {k: mksem("ss_" + k) for k in
          ["sd", "af", "evK", "Q", "nd", "ln", "g", "f", "pw"]}
    st = {k: mksem("st_" + k) for k in
          ["T", "K", "V", "KV", "Q", "ND", "T2", "FC", "T3", "gu", "t3c",
           "d", "pw"]}

    ENGS = ["sync", "vector", "scalar", "tensor", "gpsimd"]
    phase_idx = [0]

    def run_phase(fns):
        if phase_idx[0] >= PHASE_LIMIT[0]:
            phase_idx[0] += 1
            return
        k = phase_idx[0]
        with nc.Block() as block:
            for ename in ENGS:
                def mk(ename=ename):
                    def body(eng):
                        if k > 0:
                            eng.wait_ge(BAR.h, 5 * k)
                        f = fns.get(ename)
                        if f is not None:
                            f(eng)
                        if ename == "vector" and f is not None:
                            eng.memset(d16[:, :], 0.0)
                        eng.nop().then_inc(BAR.h, 1)
                    return body
                getattr(block, ename)(mk())
        BAR.n += 5
        phase_idx[0] += 1

    def aT_cols(f, c0, n):
        return aT[:, f * 2048 + c0: f * 2048 + c0 + n]

    def BQT_cols(f, c0, n):
        return BQT[:, f * 2048 + c0: f * 2048 + c0 + n]

    x_t = x_in[:, :].rearrange("(t p) d -> t p d", p=128)
    x1s_t = x1s[:, :].rearrange("(t p) d -> t p d", p=128)
    y_t = y_out[:, :].rearrange("(t p) d -> t p d", p=128)

    def p0_sync(sync):
        d1.inc(sync.dma_start(G_t[:, :], gam_d[0:1, :].broadcast_to([128, D])), 16)
        d1.inc(sync.dma_start(B_t[:, :], bet_d[0:1, :].broadcast_to([128, D])), 16)
        sync.wait_ge(d1.h, 32)

    def p0_gpsimd(g):
        gi.inc(g.iota(wbuf0[:, 0:128], [[0, 128]], channel_multiplier=1,
                      allow_small_or_imprecise_dtypes=True))
        gi.inc(g.iota(wbuf0[:, 128:256], [[1, 128]], channel_multiplier=0,
                      allow_small_or_imprecise_dtypes=True))

    def p0_vector(v):
        v.wait_ge(gi.h, 2)
        v.tensor_tensor(idf[:, :], wbuf0[:, 0:128], wbuf0[:, 128:256],
                        op=ALU.is_equal)
        v.tensor_copy(idb[:, :], idf[:, :])
        v.memset(onec[:, :], 1.0)
        v.memset(onecF[:, :], 1.0)
        v.memset(KV2[:, :], 0.0)
        v.memset(KVs[:, :], 0.0)
        v.memset(epsv[:, :], float(LN_EPS))

    run_phase({"sync": p0_sync, "gpsimd": p0_gpsimd, "vector": p0_vector})

    dPW = d1.n
    dGPW = dG.n
    NWEL = INNER * D

    def w_src(wd):
        if wd is wd_d or wd is wqd:
            return wd[:, :].rearrange("(c p) d -> c p d", p=128)
        return wd[:, :].rearrange("(c p a) d -> c p (a d)", p=128, a=4)

    def pw_gpsimd(g):
        for wdd, wss in ((wq_d, Wq_s), (wk_d, Wk_s), (wv_d, Wv_s), (wfc_d, Wfc_s)):
            dG.inc(g.dma_start(
                wss[:, :].rearrange("p (f d) -> p f d", d=D),
                wdd[:, :].rearrange("(f p) d -> p f d", p=128)), 16)
        g.wait_ge(dG.h, dGPW + 64)

    def pw_sync(sync):
        for wi, wdd in enumerate((wg_d, wu_d, wd_d)):
            src = w_src(wdd)
            for c in range(8):
                if wi * 8 + c >= 2:
                    sync.wait_ge(st["pw"].h, 64 * wi + 8 * (c - 1))
                if wi >= 1 and c <= 1:
                    sync.wait_ge(sv["pw"].h, 9 * wi + c - 1)
                d1.inc(sync.dma_start(wbuf0[:, :] if c % 2 == 0 else wbuf1[:, :],
                                      src[c]), 16)
            for c in range(8):
                sync.wait_ge(st["pw"].h, 64 * (wi + 1))
                if c >= 2:
                    sync.wait_ge(sv["pw"].h, 9 * wi + 1 + (c - 1))
                d1.inc(sync.dma_start(wbuf0[:, :] if c % 2 == 0 else wbuf1[:, :],
                                      src[c]), 16)

    def pw_scalar(s):
        for wi, wq_dst in enumerate((wqg, wqu, wqd)):
            s.wait_ge(sv["pw"].h, 9 * wi + 1)
            d3.inc(s.dma_start(sc[0:1, 2 * wi: 2 * wi + 2], scp[0:1, 0:2]), 16)
            s.wait_ge(d3.h, 176 * wi + 16)
            d3.inc(s.dma_start(
                swv[:, :], sc[0:1, 2 * wi + 1: 2 * wi + 2].broadcast_to([128, 1])), 16)
            cmv = (cmg127, cmu127, cmd127)[wi]
            d3.inc(s.dma_start(
                cmv[:, :], sc[0:1, 2 * wi: 2 * wi + 1].broadcast_to([128, 1])), 16)
            dst = w_src(wq_dst)
            for c in range(8):
                s.wait_ge(sv["pw"].h, 9 * wi + 1 + (c + 1))
                tn = tern0 if c % 2 == 0 else tern1
                d3.inc(s.dma_start(dst[c], tn[:, :]), 16)
        s.wait_ge(d3.h, 176 * 3)

    def pw_tensor(t):
        for wi in range(3):
            if wi >= 1:
                t.wait_ge(sv["pw"].h, 9 * (wi - 1) + 1)
            for c in range(8):
                t.wait_ge(ss["pw"].h, 8 * wi + c + 1)
                wa = wbuf0 if c % 2 == 0 else wbuf1
                for j in range(8):
                    st["pw"].inc(t.matmul(PS[0][0:1, 0:512], onecF[:, :],
                                          wa[:, j * 512:(j + 1) * 512],
                                          start=(c == 0 and j == 0),
                                          stop=(c == 7 and j == 7)))
        t.wait_ge(st["pw"].h, 192)

    def pw_vector(v):
        for wi in range(3):
            for c in range(8):
                v.wait_ge(d1.h, dPW + 16 * (16 * wi + c + 1))
                wb = wbuf0 if c % 2 == 0 else wbuf1
                wbi = wb[:, :].bitcast(mybir.dt.int32)
                v.tensor_scalar(wbi, wbi, 0x7fffffff, None, op0=ALU.bitwise_and)
                v.memset(vspc[:, :], 0.0)
                ss["pw"].inc(v.memset(vspc[:, :], 0.0))
            v.wait_ge(st["pw"].h, 64 * (wi + 1))
            if wi >= 1:
                v.wait_ge(d3.h, 176 * (wi - 1) + 48)
            v.tensor_reduce(mb1[0:1, 0:1], PS[0][0:1, 0:512], axis=AX.X, op=ALU.add)
            v.memset(d16[:, :], 0.0)
            v.memset(vspc[:, :], 0.0)
            v.tensor_scalar(scp[0:1, 0:1], mb1[0:1, 0:1], 1.0 / NWEL, None,
                            op0=ALU.mult)
            v.memset(d16[:, :], 0.0)
            v.memset(vspc[:, :], 0.0)
            v.tensor_scalar(scp[0:1, 0:1], scp[0:1, 0:1], float(QEPS), None,
                            op0=ALU.max)
            v.memset(d16[:, :], 0.0)
            v.memset(vspc[:, :], 0.0)
            v.reciprocal(scp[0:1, 1:2], scp[0:1, 0:1])
            v.memset(d16[:, :], 0.0)
            v.memset(vspc[:, :], 0.0)
            sv["pw"].inc(v.memset(vspc[:, :], 0.0))
            for c in range(8):
                v.wait_ge(d3.h, 176 * wi + 48)
                v.wait_ge(d1.h, dPW + 16 * (16 * wi + 8 + c + 1))
                if c >= 2:
                    v.wait_ge(d3.h, 176 * wi + 48 + 16 * (c - 1))
                wb = wbuf0 if c % 2 == 0 else wbuf1
                tn = tern0 if c % 2 == 0 else tern1
                v.tensor_scalar(wb[:, :], wb[:, :], swv[:, :], MAGIC,
                                op0=ALU.mult, op1=ALU.add)
                v.memset(vspc[:, :], 0.0)
                v.memset(vspc[:, :], 0.0)
                v.tensor_scalar(wb[:, :], wb[:, :], MAGIC, 1.0,
                                op0=ALU.subtract, op1=ALU.min)
                v.memset(vspc[:, :], 0.0)
                v.memset(vspc[:, :], 0.0)
                v.tensor_scalar(tn[:, :], wb[:, :], -1.0, None, op0=ALU.max)
                v.memset(vspc[:, :], 0.0)
                sv["pw"].inc(v.memset(vspc[:, :], 0.0))
                v.memset(vspc[:, :], 0.0)
        v.wait_ge(d3.h, 176 * 2 + 48)
        v.tensor_scalar(cmg127[:, :], cmg127[:, :], 1.0 / 127.0, None, op0=ALU.mult)
        v.tensor_scalar(cmu127[:, :], cmu127[:, :], 1.0 / 127.0, None, op0=ALU.mult)
        v.tensor_scalar(cmd127[:, :], cmd127[:, :], 1.0 / 127.0, None, op0=ALU.mult)

    run_phase({"sync": pw_sync, "scalar": pw_scalar, "tensor": pw_tensor,
               "vector": pw_vector, "gpsimd": pw_gpsimd})

    dA = d1.n
    NI = NT + 9

    def a_sync(sync):
        for i in range(NT):
            if i >= 4 and A_LIM[0] >= 2:
                sync.wait_ge(ss["af"].h, i - 3)
            d1.inc(sync.dma_start(x0r[i % 4][:, :], x_t[i]), 16)

    def a_vector(v):
        for i in range(NI):
            if i < NT and A_LIM[0] != 34:
                v.wait_ge(d1.h, dA + 16 * (i + 1))
                stx = st1 if i % 2 == 0 else MVst
                v.bn_stats(stx[:, 0:6], x0r[i % 4][:, 0:512])
                v.bn_stats(stx[:, 6:12], x0r[i % 4][:, 512:1024])
            t = i - 4
            if 0 <= t < NT and A_LIM[0] >= 3:
                v.wait_ge(st["T"].h, 2 * t + 2)
                if A_LIM[0] == 37:
                    pass
                else:
                    v.tensor_copy(
                        aT[:, :].rearrange("p (f k) -> p f k", k=2048)
                        [:, 4:8, t * 128:(t + 1) * 128],
                        PS[6][:, :].rearrange("p (f k) -> p f k", k=128))
                    v.memset(vspc[:, :], 0.0)
                    sv["evT"].inc(v.memset(vspc[:, :], 0.0))
                    v.memset(vspc[:, :], 0.0)
                    if A_LIM[0] == 38:
                        v.memset(d16[:, :], 0.0)
            else:
                v.memset(d16[:, :], 0.0)
            t = i - 3
            if 0 <= t < NT and A_LIM[0] >= 3:
                if A_LIM[0] == 37:
                    pass
                else:
                    v.wait_ge(st["T"].h, 2 * t + 1)
                    v.tensor_copy(
                        aT[:, :].rearrange("p (f k) -> p f k", k=2048)
                        [:, 0:4, t * 128:(t + 1) * 128],
                        PS[6][:, :].rearrange("p (f k) -> p f k", k=128))
                    v.memset(vspc[:, :], 0.0)
                    sv["evT"].inc(v.memset(vspc[:, :], 0.0))
                    v.memset(vspc[:, :], 0.0)
                    if A_LIM[0] == 38:
                        v.memset(d16[:, :], 0.0)
            else:
                v.memset(d16[:, :], 0.0)
            if i < NT and A_LIM[0] != 34:
                stx = st1 if i % 2 == 0 else MVst
                v.bn_aggr(MV[:, 2 * i: 2 * i + 2], stx[:, 0:12])
            t = i - 6
            if 0 <= t < NT and A_LIM[0] >= 5 and A_LIM[0] not in (31, 32):
                v.wait_ge(st["V"].h, t + 1)
                v.wait_ge(st["KV"].h, max(0, t - 1))
                v.tensor_copy(Vb[t % 2][:, 0:512], PS[2][:, :])
                v.tensor_copy(Vb[t % 2][:, 512:1024], PS[3][:, :])
                v.memset(vspc[:, :], 0.0)
                sv["evV"].inc(v.memset(vspc[:, :], 0.0))
                v.memset(vspc[:, :], 0.0)
                v.memset(vspc[:, :], 0.0)
            else:
                v.memset(d16[:, :], 0.0)
            t = i - 8
            if 0 <= t < NT and A_LIM[0] >= 6 and A_LIM[0] not in (31, 32):
                v.wait_ge(st["KV"].h, t + 1)
                kv3 = KVs[:, :].rearrange("p (h c) -> p h c", c=65)
                v.tensor_tensor(kv3[:, 0:8, 0:64], kv3[:, 0:8, 0:64],
                                PS[4][0:64, :].rearrange("p (h c) -> p h c", c=64),
                                op=ALU.add)
                v.tensor_tensor(kv3[:, 8:16, 0:64], kv3[:, 8:16, 0:64],
                                PS[5][0:64, :].rearrange("p (h c) -> p h c", c=64),
                                op=ALU.add)
                ks3 = PS[7][0:64, 0:16].rearrange("p (h e) -> p h e", e=2)
                v.tensor_tensor(kv3[:, 0:8, 64:65], kv3[:, 0:8, 64:65],
                                ks3[:, :, 0:1], op=ALU.add)
                v.tensor_tensor(kv3[:, 8:16, 64:65], kv3[:, 8:16, 64:65],
                                ks3[:, :, 1:2], op=ALU.add)
                v.memset(vspc[:, :], 0.0)
                sv["evKV"].inc(v.memset(vspc[:, :], 0.0))
                v.memset(vspc[:, :], 0.0)
                v.memset(vspc[:, :], 0.0)
            t = i - 1
            if 0 <= t < NT and A_LIM[0] != 34:
                v.wait_ge(ss["sd"].h, t + 1)
                v.reciprocal(RS[:, t: t + 1], SD[:, t: t + 1])
                v.memset(d16[:, :], 0.0)
                v.memset(vspc[:, :], 0.0)
                v.scalar_tensor_tensor(NMR[:, t: t + 1], MV[:, 2 * t: 2 * t + 1],
                                       -1.0, RS[:, t: t + 1],
                                       op0=ALU.mult, op1=ALU.mult)
            v.memset(vspc[:, :], 0.0)
            sv["stat"].inc(v.memset(vspc[:, :], 0.0))
            v.memset(vspc[:, :], 0.0)

    def a_scalar(s):
        for i in range(NI):
            t = i - 1
            if 0 <= t < NT and A_LIM[0] != 34:
                s.wait_ge(sv["stat"].h, i)
                s.activation(SD[:, t: t + 1], MV[:, 2 * t + 1: 2 * t + 2],
                             AF.Sqrt, bias=epsv[:, :])
                s.memzero(sspc[:, :])
                ss["sd"].inc(s.memzero(sspc[:, :]))
                s.memzero(sspc[:, :])
            t = i - 2
            if 0 <= t < NT and A_LIM[0] >= 2:
                if A_LIM[0] != 34:
                    s.wait_ge(sv["stat"].h, i)
                if t >= 2 and A_LIM[0] >= 3:
                    s.wait_ge(st["T"].h, 2 * t - 2)
                if A_LIM[0] == 35:
                    s.memzero(sspc[:, :])
                    ss["af"].inc(s.memzero(sspc[:, :]))
                    s.memzero(sspc[:, :])
                else:
                    s.activation(
                        atn[t % 2][:, :],
                        x0r[t % 4][:, :], AF.Identity,
                        bias=NMR[:, t: t + 1], scale=RS[:, t: t + 1])
                    s.memzero(sspc[:, :])
                    ss["af"].inc(s.memzero(sspc[:, :]))
            if A_LIM[0] == 37:
                t = i - 4
                if 0 <= t < NT:
                    s.wait_ge(st["T"].h, 2 * t + 2)
                    cp = s.activation(
                        aT[:, :].rearrange("p (f k) -> p f k", k=2048)
                        [:, 4:8, t * 128:(t + 1) * 128],
                        PS[6][:, :].rearrange("p (f k) -> p f k", k=128),
                        AF.Identity)
                    sv["evT"].inc(cp)
                    v.memset(vspc[:, :], 0.0)
                t = i - 3
                if 0 <= t < NT:
                    s.wait_ge(st["T"].h, 2 * t + 1)
                    cp = s.activation(
                        aT[:, :].rearrange("p (f k) -> p f k", k=2048)
                        [:, 0:4, t * 128:(t + 1) * 128],
                        PS[6][:, :].rearrange("p (f k) -> p f k", k=128),
                        AF.Identity)
                    sv["evT"].inc(cp)
                    v.memset(vspc[:, :], 0.0)
            t = i - 6
            if 0 <= t < NT and A_LIM[0] >= 4 and A_LIM[0] not in (31, 32):
                s.wait_ge(st["K"].h, t + 1)
                s.wait_ge(st["KV"].h, max(0, t - 1))
                s.activation(Kb[t % 2][:, 0:512], PS[0][:, :], AF.Sigmoid)
                s.activation(Kb[t % 2][:, 512:1024], PS[1][:, :], AF.Sigmoid)
                s.memzero(sspc[:, :])
                ss["evK"].inc(s.memzero(sspc[:, :]))

    def a_tensor(te):
        for i in range(NI):
            t = i - 3
            if 0 <= t < NT and A_LIM[0] >= 3:
                te.wait_ge(ss["af"].h, t + 1)
                te.wait_ge(sv["evT"].h, 2 * t)
                for j in range(4):
                    mm = te.matmul(PS[6][:, j * 128:(j + 1) * 128],
                                   atn[t % 2][:, j * 128:(j + 1) * 128], idb[:, :],
                                   start=True, stop=True)
                st["T"].inc(mm)
                te.wait_ge(sv["evT"].h, 2 * t + 1)
                for j in range(4):
                    mm = te.matmul(PS[6][:, j * 128:(j + 1) * 128],
                                   atn[t % 2][:, (4 + j) * 128:(5 + j) * 128],
                                   idb[:, :], start=True, stop=True)
                st["T"].inc(mm)
            t = i - 5
            if 0 <= t < NT and A_LIM[0] >= 4 and A_LIM[0] not in (31, 32):
                te.wait_ge(sv["evT"].h, 2 * t + 2)
                te.wait_ge(ss["evK"].h, t)
                for n in range(2):
                    for f in range(8):
                        mm = te.matmul(PS[n][:, :], aT_cols(f, t * 128, 128),
                                       Wk_s[:, f * 1024 + n * 512: f * 1024 + (n + 1) * 512],
                                       start=(f == 0), stop=(f == 7))
                st["K"].inc(mm)
            t = i - 5
            if 0 <= t < NT and A_LIM[0] >= 5 and A_LIM[0] not in (31, 32):
                te.wait_ge(sv["evV"].h, t)
                for n in range(2):
                    for f in range(8):
                        mm = te.matmul(PS[2 + n][:, :], aT_cols(f, t * 128, 128),
                                       Wv_s[:, f * 1024 + n * 512: f * 1024 + (n + 1) * 512],
                                       start=(f == 0), stop=(f == 7))
                st["V"].inc(mm)
            t = i - 7
            if 0 <= t < NT and A_LIM[0] >= 6 and A_LIM[0] not in (31, 32):
                te.wait_ge(ss["evK"].h, t + 1)
                te.wait_ge(sv["evV"].h, t + 1)
                te.wait_ge(sv["evKV"].h, t)
                for h in range(8):
                    te.matmul(PS[4][0:64, h * 64:(h + 1) * 64],
                              Kb[t % 2][:, h * 64:(h + 1) * 64],
                              Vb[t % 2][:, h * 64:(h + 1) * 64],
                              start=True, stop=True)
                    te.matmul(PS[5][0:64, h * 64:(h + 1) * 64],
                              Kb[t % 2][:, (h + 8) * 64:(h + 9) * 64],
                              Vb[t % 2][:, (h + 8) * 64:(h + 9) * 64],
                              start=True, stop=True)
                    te.matmul(PS[7][0:64, 2 * h: 2 * h + 1],
                              Kb[t % 2][:, h * 64:(h + 1) * 64],
                              onec[:, :], start=True, stop=True)
                    mm = te.matmul(PS[7][0:64, 2 * h + 1: 2 * h + 2],
                                   Kb[t % 2][:, (h + 8) * 64:(h + 9) * 64],
                                   onec[:, :], start=True, stop=True)
                st["KV"].inc(mm)
        te.wait_ge(st["KV"].h, NT)

    run_phase({"sync": a_sync, "vector": a_vector, "scalar": a_scalar,
               "tensor": a_tensor})

    dCC = d1.n
    dGCC = dG.n
    svlnCC = sv["ln"].n

    def cc_sync(sync):
        d1.inc(sync.dma_start(cc_in[:, :], KVs[:, :]), 16)
        sync.wait_ge(cs.h, 1)
        sync.wait_ge(sv["ln"].h, svlnCC + 1)
        ev = cc_out[:, :].rearrange("p (c e j) -> p c e j", e=2, j=65)
        st0 = ndb[0][:, :].rearrange("p (c k) -> p c k", k=130)
        d1.inc(sync.dma_start(st0[0:64, :, 0:65], ev[:, :, 0, :]), 16)
        d1.inc(sync.dma_start(st0[64:128, :, 65:130], ev[:, :, 1, :]), 16)
        if DBG[0]:
            sync.wait_ge(sv["ln"].h, svlnCC + 2)
            d1.inc(sync.dma_start(dbgk2[:, :], ndb[0][:, :]), 16)
            d1.inc(sync.dma_start(dbgkv[:, :], KVs[:, :]), 16)
            d1.inc(sync.dma_start(dbgmv[:, 0:32], MV[:, :]), 16)
            d1.inc(sync.dma_start(dbgmv[:, 32:48], RS[:, :]), 16)
            d1.inc(sync.dma_start(dbgmv[:, 48:64], NMR[:, :]), 16)

    def cc_gpsimd(g):
        g.wait_ge(d1.h, dCC + 16)
        cs.inc(g.collective_compute(
            "AllReduce", ALU.add,
            replica_groups=[[0, 1, 2, 3], [4, 5, 6, 7]],
            ins=[cc_in.ap().opt()], outs=[cc_out.ap().opt()]))

    def cc_vector(v):
        v.memset(ndb[0][:, :], 0.0)
        v.memset(vspc[:, :], 0.0)
        sv["ln"].inc(v.memset(vspc[:, :], 0.0))
        v.wait_ge(d1.h, dCC + 48)
        v.tensor_copy(KV2[:, :], ndb[0][:, :])
        v.memset(vspc[:, :], 0.0)
        sv["ln"].inc(v.memset(vspc[:, :], 0.0))

    run_phase({"sync": cc_sync, "gpsimd": cc_gpsimd, "vector": cc_vector})

    dB = d1.n
    d3B = d3.n
    ssafB = ss["af"].n
    sslnB = ss["ln"].n
    svlnB = sv["ln"].n

    def b1_sync(sync):
        for t in range(NT):
            if t >= 4:
                sync.wait_ge(d3.h, d3B + 16 * (t - 3))
            d1.inc(sync.dma_start(x0r[t % 4][:, :], x_t[t]), 16)

    def b1_tensor(te):
        for t in range(NT):
            te.wait_ge(ss["Q"].h, 2 * t)
            for m in (0, 4, 1, 5, 2, 6, 3, 7):
                bank = m // 4
                for f in range(8):
                    mmq = te.matmul(
                        PS[bank][:, (m % 4) * 128:(m % 4 + 1) * 128],
                        Wq_s[:, f * 1024 + m * 128: f * 1024 + (m + 1) * 128],
                        aT_cols(f, t * 128, 128),
                        start=(f == 0), stop=(f == 7))
            st["Q"].inc(mmq)
            te.wait_ge(ss["Q"].h, 2 * t + 2)
            for c in range(8):
                te.wait_ge(ss["nd"].h, max(0, 8 * t + c - 1))
                st["ND"].inc(te.matmul(PS[2 + c % 2][:, 0:130],
                                       QTt[t % 2][:, c * 128:(c + 1) * 128],
                                       KV2[:, c * 130:(c + 1) * 130],
                                       start=True, stop=True))
            te.wait_ge(sv["nd"].h, t + 1)
            te.wait_ge(sv["T3"].h, 2 * t)
            for f in range(8):
                bank = 6 + f % 2
                mm = te.matmul(PS[bank][:, (f // 2) * 128:(f // 2 + 1) * 128],
                               atn[t % 2][:, f * 128:(f + 1) * 128], idb[:, :],
                               start=True, stop=True)
            st["T2"].inc(mm)
            te.wait_ge(sv["T3"].h, 2 * t + 1)
            te.wait_ge(sv["y"].h, t)
            for n in range(2):
                for f in range(8):
                    mm = te.matmul(PS[4 + n][:, :],
                                   atnT[t % 2][:, f * 128:(f + 1) * 128],
                                   Wfc_s[:, f * 1024 + n * 512: f * 1024 + (n + 1) * 512],
                                   start=(f == 0), stop=(f == 7))
            st["FC"].inc(mm)
            te.wait_ge(sv["q"].h, t + 1)
            te.wait_ge(sv["T3"].h, 2 * t + 1)
            for f in range(8):
                bank = 6 + f % 2
                mm = te.matmul(PS[bank][:, (f // 2) * 128:(f // 2 + 1) * 128],
                               bqb[t % 2][:, f * 128:(f + 1) * 128], idb[:, :],
                               start=True, stop=True)
            st["T3"].inc(mm)
        te.wait_ge(st["T3"].h, NT)

    def b1_scalar(s):
        for t in range(NT):
            s.wait_ge(st["Q"].h, t + 1)
            s.wait_ge(st["ND"].h, max(0, 8 * t))
            s.activation(QTt[t % 2][:, 0:512], PS[0][:, :], AF.Sigmoid)
            s.memzero(sspc[:, :])
            ss["Q"].inc(s.memzero(sspc[:, :]))
            s.activation(QTt[t % 2][:, 512:1024], PS[1][:, :], AF.Sigmoid)
            s.memzero(sspc[:, :])
            ss["Q"].inc(s.memzero(sspc[:, :]))
            for c in range(8):
                s.wait_ge(st["ND"].h, 8 * t + c + 1)
                if c == 0:
                    s.wait_ge(sv["nd"].h, max(0, t - 1))
                ap = s.activation(ndb[t % 2][:, c * 130:(c + 1) * 130],
                                  PS[2 + c % 2][:, 0:130], AF.Identity)
                if c == 7:
                    s.memzero(sspc[:, :])
                    ss["nd"].inc(s.memzero(sspc[:, :]))
                else:
                    ss["nd"].inc(ap)
            s.wait_ge(d1.h, dB + 16 * (t + 1))
            s.wait_ge(sv["y"].h, t)
            s.activation(af0[:, :] if t % 2 == 0 else af1[:, :],
                         x0r[t % 4][:, :], AF.Identity,
                         bias=NMR[:, t: t + 1], scale=RS[:, t: t + 1])
            s.memzero(sspc[:, :])
            ss["af"].inc(s.memzero(sspc[:, :]))
            s.memzero(sspc[:, :])
            for k in range(3):
                s.wait_ge(sv["ln"].h, svlnB + 6 * t + 2 * k + 1)
                s.activation(sm2[:, 0:1], sm1[:, 1:2], AF.Sqrt, bias=epsv[:, :])
                s.memzero(sspc[:, :])
                ss["ln"].inc(s.memzero(sspc[:, :]))
                s.memzero(sspc[:, :])
                s.wait_ge(sv["ln"].h, svlnB + 6 * t + 2 * k + 2)
                if k == 0:
                    s.activation(t1b[:, :], ytb[:, :], AF.Identity,
                                 bias=sm2[:, 3:4], scale=sm2[:, 1:2])
                elif k == 1:
                    s.activation(t1b[:, :], ytb[:, :], AF.Identity,
                                 bias=sm2[:, 3:4], scale=sm2[:, 1:2])
                else:
                    s.activation(btb[:, :], x0r[t % 4][:, :], AF.Identity,
                                 bias=sm2[:, 3:4], scale=sm2[:, 1:2])
                s.memzero(sspc[:, :])
                ss["ln"].inc(s.memzero(sspc[:, :]))
                s.memzero(sspc[:, :])
            s.wait_ge(sv["q"].h, t + 1)
            d3.inc(s.dma_start(x1s_t[t], x0r[t % 4][:, :]), 16)

    def b1_vector(v):
        for t in range(NT):
            v.wait_ge(ss["nd"].h, 8 * (t + 1))
            v.wait_ge(st["T2"].h, max(0, t - 1))
            nd3 = ndb[t % 2][:, :].rearrange("p (c k) -> p c k", k=130)
            v.tensor_scalar(d16[:, :].rearrange("p (c e) -> p c e", e=2),
                            nd3[:, :, 64:130:65],
                            float(DEN_EPS), None, op0=ALU.add)
            v.memset(vspc[:, :], 0.0)
            v.memset(vspc[:, :], 0.0)
            v.reciprocal(r16[:, :], d16[:, :])
            v.memset(vspc[:, :], 0.0)
            v.memset(vspc[:, :], 0.0)
            at3 = atn[t % 2][:, :].rearrange("p (c e j) -> p c e j", e=2, j=64)
            r3 = r16[:, :].rearrange("p (c e) -> p c e", e=2)
            v.scalar_tensor_tensor(
                at3[:, :, 0, :],
                r3[:, :, 0:1].broadcast_to([128, 8, 64]), 1.0,
                nd3[:, :, 0:64], op0=ALU.mult, op1=ALU.mult)
            v.scalar_tensor_tensor(
                at3[:, :, 1, :],
                r3[:, :, 1:2].broadcast_to([128, 8, 64]), 1.0,
                nd3[:, :, 65:129], op0=ALU.mult, op1=ALU.mult)
            v.memset(vspc[:, :], 0.0)
            sv["nd"].inc(v.memset(vspc[:, :], 0.0))
            v.memset(vspc[:, :], 0.0)
            v.wait_ge(st["T2"].h, t + 1)
            v.wait_ge(st["FC"].h, max(0, t - 1))
            for par, bank in ((0, 6), (1, 7)):
                v.tensor_copy(
                    atnT[t % 2][:, :].rearrange("p (f k) -> p f k", k=128)
                    [:, par:8:2, :],
                    PS[bank][:, :].rearrange("p (f k) -> p f k", k=128))
            v.memset(vspc[:, :], 0.0)
            sv["T3"].inc(v.memset(vspc[:, :], 0.0))
            v.wait_ge(st["FC"].h, t + 1)
            v.wait_ge(ss["af"].h, ssafB + t + 1)
            af = af0 if t % 2 == 0 else af1
            v.tensor_tensor(ytb[:, 0:512], PS[4][:, :], af[:, 0:512], op=ALU.add)
            v.tensor_tensor(ytb[:, 512:1024], PS[5][:, :],
                            af[:, 512:1024], op=ALU.add)
            v.memset(vspc[:, :], 0.0)
            sv["y"].inc(v.memset(vspc[:, :], 0.0))
            if t >= 1:
                v.wait_ge(ss["ln"].h, sslnB + 6 * (t - 1) + 5)
            v.bn_stats(st1[:, 0:6], ytb[:, 0:512])
            v.bn_stats(st1[:, 6:12], ytb[:, 512:1024])
            v.memset(d16[:, :], 0.0)
            v.memset(vspc[:, :], 0.0)
            v.memset(d16[:, :], 0.0)
            v.bn_aggr(sm1[:, 0:2], st1[:, 0:12])
            v.memset(d16[:, :], 0.0)
            v.memset(vspc[:, :], 0.0)
            sv["ln"].inc(v.memset(vspc[:, :], 0.0))
            v.wait_ge(ss["ln"].h, sslnB + 6 * t + 1)
            v.reciprocal(sm2[:, 1:2], sm2[:, 0:1])
            v.memset(d16[:, :], 0.0)
            v.memset(vspc[:, :], 0.0)
            v.scalar_tensor_tensor(sm2[:, 3:4], sm1[:, 0:1], -1.0, sm2[:, 1:2],
                                   op0=ALU.mult, op1=ALU.mult)
            v.memset(vspc[:, :], 0.0)
            sv["ln"].inc(v.memset(vspc[:, :], 0.0))
            v.wait_ge(ss["ln"].h, sslnB + 6 * t + 2)
            v.tensor_tensor(ytb[:, :], t1b[:, :], G_t[:, :], op=ALU.mult)
            v.tensor_tensor(ytb[:, :], ytb[:, :], B_t[:, :], op=ALU.add)
            v.memset(vspc[:, :], 0.0)
            v.memset(vspc[:, :], 0.0)
            v.bn_stats(st1[:, 0:6], ytb[:, 0:512])
            v.bn_stats(st1[:, 6:12], ytb[:, 512:1024])
            v.memset(d16[:, :], 0.0)
            v.memset(vspc[:, :], 0.0)
            v.memset(d16[:, :], 0.0)
            v.bn_aggr(sm1[:, 0:2], st1[:, 0:12])
            v.memset(d16[:, :], 0.0)
            v.memset(vspc[:, :], 0.0)
            sv["ln"].inc(v.memset(vspc[:, :], 0.0))
            v.wait_ge(ss["ln"].h, sslnB + 6 * t + 3)
            v.reciprocal(sm2[:, 1:2], sm2[:, 0:1])
            v.memset(d16[:, :], 0.0)
            v.memset(vspc[:, :], 0.0)
            v.scalar_tensor_tensor(sm2[:, 3:4], sm1[:, 0:1], -1.0, sm2[:, 1:2],
                                   op0=ALU.mult, op1=ALU.mult)
            v.memset(vspc[:, :], 0.0)
            sv["ln"].inc(v.memset(vspc[:, :], 0.0))
            v.wait_ge(ss["ln"].h, sslnB + 6 * t + 4)
            v.tensor_tensor(x0r[t % 4][:, :], t1b[:, :], x0r[t % 4][:, :],
                            op=ALU.add)
            v.memset(vspc[:, :], 0.0)
            v.memset(vspc[:, :], 0.0)
            v.bn_stats(st1[:, 0:6], x0r[t % 4][:, 0:512])
            v.bn_stats(st1[:, 6:12], x0r[t % 4][:, 512:1024])
            v.memset(d16[:, :], 0.0)
            v.memset(vspc[:, :], 0.0)
            v.memset(d16[:, :], 0.0)
            v.bn_aggr(sm1[:, 0:2], st1[:, 0:12])
            v.memset(d16[:, :], 0.0)
            v.memset(vspc[:, :], 0.0)
            sv["ln"].inc(v.memset(vspc[:, :], 0.0))
            v.wait_ge(ss["ln"].h, sslnB + 6 * t + 5)
            v.reciprocal(sm2[:, 1:2], sm2[:, 0:1])
            v.memset(d16[:, :], 0.0)
            v.memset(vspc[:, :], 0.0)
            v.scalar_tensor_tensor(sm2[:, 3:4], sm1[:, 0:1], -1.0, sm2[:, 1:2],
                                   op0=ALU.mult, op1=ALU.mult)
            v.memset(vspc[:, :], 0.0)
            sv["ln"].inc(v.memset(vspc[:, :], 0.0))
            v.wait_ge(ss["ln"].h, sslnB + 6 * t + 6)
            v.reduce_max(mb1[:, :], btb[:, :], axis=AX.X, apply_absolute_value=True)
            v.memset(d16[:, :], 0.0)
            v.memset(vspc[:, :], 0.0)
            v.tensor_scalar(MMB[:, t: t + 1], mb1[:, :], float(QEPS), None,
                            op0=ALU.max)
            v.memset(d16[:, :], 0.0)
            v.memset(vspc[:, :], 0.0)
            v.reciprocal(rb1[:, :], MMB[:, t: t + 1])
            v.memset(d16[:, :], 0.0)
            v.memset(vspc[:, :], 0.0)
            v.tensor_scalar(SVB[:, t: t + 1], rb1[:, :], 127.0, None, op0=ALU.mult)
            v.tensor_tensor(CG[:, t: t + 1], MMB[:, t: t + 1], cmg127[:, :],
                            op=ALU.mult)
            v.tensor_tensor(CU[:, t: t + 1], MMB[:, t: t + 1], cmu127[:, :],
                            op=ALU.mult)
            v.wait_ge(st["T3"].h, max(0, t - 1))
            v.tensor_scalar(t1b[:, :], btb[:, :], SVB[:, t: t + 1], MAGIC,
                            op0=ALU.mult, op1=ALU.add)
            v.memset(vspc[:, :], 0.0)
            v.memset(vspc[:, :], 0.0)
            v.tensor_scalar(bqb[t % 2][:, :], t1b[:, :], MAGIC, None,
                            op0=ALU.subtract)
            v.memset(vspc[:, :], 0.0)
            sv["q"].inc(v.memset(vspc[:, :], 0.0))
            v.memset(vspc[:, :], 0.0)
            v.wait_ge(st["T3"].h, t + 1)
            for par, bank in ((0, 6), (1, 7)):
                v.tensor_copy(
                    BQT[:, :].rearrange("p (f k) -> p f k", k=2048)
                    [:, par:8:2, t * 128:(t + 1) * 128],
                    PS[bank][:, :].rearrange("p (f k) -> p f k", k=128))
            v.memset(vspc[:, :], 0.0)
            sv["T3"].inc(v.memset(vspc[:, :], 0.0))

    run_phase({"sync": b1_sync, "tensor": b1_tensor, "scalar": b1_scalar,
               "vector": b1_vector})

    for Hf in range(2):
        T0 = 8 * Hf
        stb_gu = st["gu"].n
        ssb_g = ss["g"].n
        svb_h = sv["h"].n
        d1b = d1.n

        def c1_sync(sync, stb_gu=stb_gu, Hf=Hf):
            if DBG[0] and Hf == 0:
                d1.inc(sync.dma_start(dbgnd[:, :], ndb[1][:, :]), 16)
                d1.inc(sync.dma_start(dbgb1[:, 0:1024], ytb[:, :]), 16)
                d1.inc(sync.dma_start(dbgb1[:, 1024:2048], t1b[:, :]), 16)
                d1.inc(sync.dma_start(dbgb1[:, 2048:3072], btb[:, :]), 16)
                d1.inc(sync.dma_start(dbgb1[:, 3072:4096], x0r[3][:, :]), 16)
            for I in range(8):
                if I >= 2:
                    sync.wait_ge(st["gu"].h, stb_gu + 8 * (I - 1))
                wgt = wgt0 if I % 2 == 0 else wgt1
                wut = wut0 if I % 2 == 0 else wut1
                for f in range(8):
                    d1.inc(sync.dma_start(
                        wgt[:, f * 512:(f + 1) * 512],
                        wqg[I * 512:(I + 1) * 512, f * 128:(f + 1) * 128],
                        transpose=True), 16)
                    d1.inc(sync.dma_start(
                        wut[:, f * 512:(f + 1) * 512],
                        wqu[I * 512:(I + 1) * 512, f * 128:(f + 1) * 128],
                        transpose=True), 16)

        def c1_tensor(te, T0=T0, ssb_g=ssb_g, svb_h=svb_h, d1b=d1b):
            for I in range(8):
                te.wait_ge(d1.h, d1b + 16 * 16 * (I + 1))
                wgt = wgt0 if I % 2 == 0 else wgt1
                wut = wut0 if I % 2 == 0 else wut1
                for th in range(8):
                    t = T0 + th
                    idx = 8 * I + th
                    if idx >= 2:
                        te.wait_ge(ss["g"].h, ssb_g + idx - 1)
                        te.wait_ge(sv["h"].h, svb_h + idx - 1)
                    for f in range(8):
                        te.matmul(PS[th % 2][:, :], BQT_cols(f, t * 128, 128),
                                  wgt[:, f * 512:(f + 1) * 512],
                                  start=(f == 0), stop=(f == 7))
                    for f in range(8):
                        mm = te.matmul(PS[2 + th % 2][:, :], BQT_cols(f, t * 128, 128),
                                       wut[:, f * 512:(f + 1) * 512],
                                       start=(f == 0), stop=(f == 7))
                    st["gu"].inc(mm)
            te.wait_ge(st["gu"].h, stb_gu + 64)

        def c1_scalar(s, T0=T0, stb_gu=stb_gu, svb_h=svb_h):
            for I in range(8):
                for th in range(8):
                    t = T0 + th
                    idx = 8 * I + th
                    s.wait_ge(st["gu"].h, stb_gu + idx + 1)
                    if idx >= 2:
                        s.wait_ge(sv["h"].h, svb_h + idx - 1)
                    gsb = gsb0 if th % 2 == 0 else gsb1
                    s.activation(gsb[:, :], PS[th % 2][:, :],
                                 AF.Silu, scale=CG[:, t: t + 1])
                    s.memzero(sspc[:, :])
                    ss["g"].inc(s.memzero(sspc[:, :]))

        def c1_vector(v, T0=T0, ssb_g=ssb_g):
            pend = None
            for I in range(8):
                for th in range(8):
                    t = T0 + th
                    idx = 8 * I + th
                    v.wait_ge(ss["g"].h, ssb_g + idx + 1)
                    gsb = gsb0 if th % 2 == 0 else gsb1
                    v.scalar_tensor_tensor(
                        H_s[:, th * 4096 + I * 512: th * 4096 + (I + 1) * 512],
                        PS[2 + th % 2][:, :], CU[:, t: t + 1], gsb[:, :],
                        op0=ALU.mult, op1=ALU.mult)
                    v.memset(vspc[:, :], 0.0)
                    sv["h"].inc(v.memset(vspc[:, :], 0.0))
                    v.memset(vspc[:, :], 0.0)
                    if pend is not None:
                        pth, pI = pend
                        v.reduce_max(
                            MH8[:, pth * 8 + pI: pth * 8 + pI + 1],
                            H_s[:, pth * 4096 + pI * 512: pth * 4096 + (pI + 1) * 512],
                            axis=AX.X, apply_absolute_value=True)
                    pend = (th, I)
            pth, pI = pend
            v.reduce_max(MH8[:, pth * 8 + pI: pth * 8 + pI + 1],
                         H_s[:, pth * 4096 + pI * 512: pth * 4096 + (pI + 1) * 512],
                         axis=AX.X, apply_absolute_value=True)
            v.memset(d16[:, :], 0.0)
            for th in range(8):
                t = T0 + th
                v.reduce_max(mb1[:, :], MH8[:, th * 8:(th + 1) * 8], axis=AX.X)
                v.memset(d16[:, :], 0.0)
                v.memset(vspc[:, :], 0.0)
                v.tensor_scalar(mb1[:, :], mb1[:, :], float(QEPS), None,
                                op0=ALU.max)
                v.memset(d16[:, :], 0.0)
                v.memset(vspc[:, :], 0.0)
                v.reciprocal(rb1[:, :], mb1[:, :])
                v.memset(d16[:, :], 0.0)
                v.memset(vspc[:, :], 0.0)
                v.tensor_scalar(SH[:, t: t + 1], rb1[:, :], 127.0, None,
                                op0=ALU.mult)
                v.tensor_tensor(CD[:, t: t + 1], mb1[:, :], cmd127[:, :],
                                op=ALU.mult)
            v.memset(vspc[:, :], 0.0)
            sv["h"].inc(v.memset(d16[:, :], 0.0))
            v.memset(vspc[:, :], 0.0)

        def c1_gpsimd(g, Hf=Hf):
            if DBG[0] and Hf == 0:
                dG.inc(g.dma_start(dbgb2[:, 0:1024], atn[1][:, :]), 16)
                dG.inc(g.dma_start(dbgb2[:, 1024:2048], atnT[1][:, :]), 16)
                dG.inc(g.dma_start(dbgb2[:, 2048:3072], bqb[1][:, :]), 16)
                dG.inc(g.dma_start(dbgb2[:, 3072:4096], QTt[1][:, :]), 16)
                dG.inc(g.dma_start(dbgb3[:, :], af1[:, :]), 16)
                dG.inc(g.dma_start(dbgwfc[:, :], Wfc_s[:, :]), 16)
                g.wait_ge(dG.h, dG.n)

        run_phase({"sync": c1_sync, "tensor": c1_tensor, "scalar": c1_scalar,
                   "vector": c1_vector, "gpsimd": c1_gpsimd})

        svb_hq = sv["hq"].n
        stb_t3c = st["t3c"].n
        svb_t3c = sv["t3c"].n

        def c2a_vector(v, T0=T0, stb_t3c=stb_t3c):
            for th in range(8):
                t = T0 + th
                if th >= 2:
                    v.wait_ge(st["t3c"].h, stb_t3c + 8 * (th - 1))
                for j in range(8):
                    v.tensor_scalar(
                        hq1s[:, :],
                        H_s[:, th * 4096 + j * 512: th * 4096 + (j + 1) * 512],
                        SH[:, t: t + 1], MAGIC, op0=ALU.mult, op1=ALU.add)
                    v.memset(vspc[:, :], 0.0)
                    v.memset(vspc[:, :], 0.0)
                    cp = v.tensor_scalar(HQb[th % 2][:, j * 512:(j + 1) * 512],
                                         hq1s[:, :], MAGIC, None, op0=ALU.subtract)
                v.memset(vspc[:, :], 0.0)
                sv["hq"].inc(v.memset(vspc[:, :], 0.0))
                v.memset(vspc[:, :], 0.0)
                v.memset(vspc[:, :], 0.0)
                for r in range(4):
                    v.wait_ge(st["t3c"].h, stb_t3c + 8 * th + 2 * r + 2)
                    for par, bank in ((0, 6), (1, 7)):
                        v.tensor_copy(
                            HQT[:, :].rearrange("p (i k) -> p i k", k=1024)
                            [:, 8 * r + par:8 * r + 8:2, th * 128:(th + 1) * 128],
                            PS[bank][:, :].rearrange("p (i k) -> p i k", k=128))
                        v.memset(vspc[:, :], 0.0)
                        sv["t3c"].inc(v.memset(vspc[:, :], 0.0))

        def c2a_tensor(te, svb_hq=svb_hq, svb_t3c=svb_t3c,
                       stb_t3c=stb_t3c):
            for th in range(8):
                te.wait_ge(sv["hq"].h, svb_hq + th + 1)
                for r in range(4):
                    if 8 * th + 2 * r > 0:
                        te.wait_ge(sv["t3c"].h, svb_t3c + 8 * th + 2 * r)
                    for s in range(4):
                        mm6 = te.matmul(PS[6][:, s * 128:(s + 1) * 128],
                                        HQb[th % 2][:, (8 * r + 2 * s) * 128:(8 * r + 2 * s + 1) * 128],
                                        idb[:, :], start=True, stop=True)
                        mm7 = te.matmul(PS[7][:, s * 128:(s + 1) * 128],
                                        HQb[th % 2][:, (8 * r + 2 * s + 1) * 128:(8 * r + 2 * s + 2) * 128],
                                        idb[:, :], start=True, stop=True)
                    st["t3c"].inc(mm6)
                    st["t3c"].inc(mm7)
            te.wait_ge(st["t3c"].h, stb_t3c + 64)

        run_phase({"vector": c2a_vector, "tensor": c2a_tensor})

        d1b2 = d1.n
        stb_d = st["d"].n
        ssb_f = ss["f"].n
        svb_x2 = sv["x2"].n
        ssb_ln = ss["ln"].n
        svb_ln = sv["ln"].n
        d3b2 = d3.n

        def c2b_sync(sync, T0=T0, svb_x2=svb_x2):
            for i in range(32):
                d1.inc(sync.dma_start(
                    WdT[:, i * 1024:(i + 1) * 1024],
                    wqd[0:1024, i * 128:(i + 1) * 128], transpose=True), 16)
            for th in range(8):
                t = T0 + th
                sync.wait_ge(d3.h, d3B + 16 * (t + 1))
                if th >= 2:
                    sync.wait_ge(sv["x2"].h, svb_x2 + th - 1)
                d1.inc(sync.dma_start(x1t[th % 2][:, :], x1s_t[t]), 16)

        def c2b_tensor(te, d1b2=d1b2, ssb_f=ssb_f, stb_d=stb_d):
            te.wait_ge(d1.h, d1b2 + 16 * 32)
            for th in range(8):
                for n in range(2):
                    idx = 2 * th + n
                    if idx >= 2:
                        te.wait_ge(ss["f"].h, ssb_f + idx - 1)
                    for i in range(32):
                        mm = te.matmul(
                            PS[n][:, :],
                            HQT[:, i * 1024 + th * 128: i * 1024 + (th + 1) * 128],
                            WdT[:, i * 1024 + n * 512: i * 1024 + (n + 1) * 512],
                            start=(i == 0), stop=(i == 31))
                    st["d"].inc(mm)
            te.wait_ge(st["d"].h, stb_d + 16)

        def c2b_scalar(s, T0=T0, stb_d=stb_d, ssb_ln=ssb_ln, svb_ln=svb_ln,
                       svb_x2=svb_x2):
            for th in range(8):
                t = T0 + th
                ft = ftb[th % 2]
                for n in range(2):
                    s.wait_ge(st["d"].h, stb_d + 2 * th + n + 1)
                    if n == 0 and th >= 2:
                        s.wait_ge(sv["x2"].h, svb_x2 + th - 1)
                    s.activation(ft[:, n * 512:(n + 1) * 512],
                                 PS[n][:, :], AF.Identity,
                                 scale=CD[:, t: t + 1])
                    s.memzero(sspc[:, :])
                    ss["f"].inc(s.memzero(sspc[:, :]))
                s.wait_ge(sv["ln"].h, svb_ln + 2 * th + 1)
                s.activation(sm2[:, 0:1], sm1[:, 1:2], AF.Sqrt, bias=epsv[:, :])
                s.memzero(sspc[:, :])
                ss["ln"].inc(s.memzero(sspc[:, :]))
                s.memzero(sspc[:, :])
                s.wait_ge(sv["ln"].h, svb_ln + 2 * th + 2)
                s.wait_ge(sv["x2"].h, svb_x2 + th)
                s.activation(t4b[:, :], ft[:, :], AF.Identity,
                             bias=sm2[:, 3:4], scale=sm2[:, 1:2])
                s.memzero(sspc[:, :])
                ss["ln"].inc(s.memzero(sspc[:, :]))
                s.memzero(sspc[:, :])

        def c2b_vector(v, ssb_f=ssb_f, ssb_ln=ssb_ln, d1b2=d1b2,
                       d3b2v=dOUT.n):
            for th in range(8):
                ft = ftb[th % 2]
                v.wait_ge(ss["f"].h, ssb_f + 2 * th + 2)
                if th >= 1:
                    v.wait_ge(ss["ln"].h, ssb_ln + 2 * (th - 1) + 1)
                v.bn_stats(st1[:, 0:6], ft[:, 0:512])
                v.bn_stats(st1[:, 6:12], ft[:, 512:1024])
                v.memset(d16[:, :], 0.0)
                v.memset(vspc[:, :], 0.0)
                v.memset(d16[:, :], 0.0)
                v.bn_aggr(sm1[:, 0:2], st1[:, 0:12])
                v.memset(d16[:, :], 0.0)
                v.memset(vspc[:, :], 0.0)
                sv["ln"].inc(v.memset(vspc[:, :], 0.0))
                v.memset(vspc[:, :], 0.0)
                v.wait_ge(ss["ln"].h, ssb_ln + 2 * th + 1)
                v.reciprocal(sm2[:, 1:2], sm2[:, 0:1])
                v.memset(d16[:, :], 0.0)
                v.memset(vspc[:, :], 0.0)
                v.scalar_tensor_tensor(sm2[:, 3:4], sm1[:, 0:1], -1.0,
                                       sm2[:, 1:2], op0=ALU.mult, op1=ALU.mult)
                v.memset(vspc[:, :], 0.0)
                sv["ln"].inc(v.memset(vspc[:, :], 0.0))
                v.memset(vspc[:, :], 0.0)
                v.wait_ge(ss["ln"].h, ssb_ln + 2 * th + 2)
                v.wait_ge(d1.h, d1b2 + 16 * 32 + 16 * (th + 1))
                if th >= 2:
                    v.wait_ge(dOUT.h, d3b2v + 16 * (th - 1))
                v.tensor_tensor(outb[th % 2][:, :], t4b[:, :],
                                x1t[th % 2][:, :], op=ALU.add)
                v.memset(vspc[:, :], 0.0)
                sv["x2"].inc(v.memset(vspc[:, :], 0.0))
                v.memset(vspc[:, :], 0.0)

        def c2b_gpsimd(g, T0=T0, svb_x2=svb_x2, dob=dOUT.n):
            for th in range(8):
                t = T0 + th
                g.wait_ge(sv["x2"].h, svb_x2 + th + 1)
                dOUT.inc(g.dma_start(y_t[t], outb[th % 2][:, :]), 16)
            g.wait_ge(dOUT.h, dob + 16 * 8)

        run_phase({"sync": c2b_sync, "tensor": c2b_tensor, "scalar": c2b_scalar,
                   "vector": c2b_vector, "gpsimd": c2b_gpsimd})

    ctx.close()
    return nc


_NC_CACHE = {}


def kernel(**inputs):
    x = np.ascontiguousarray(inputs["L0_attn_inputs"])
    wq = np.ascontiguousarray(inputs["L0_attn_W_Q"])
    wk = np.ascontiguousarray(inputs["L0_attn_W_K"])
    wv = np.ascontiguousarray(inputs["L0_attn_W_V"])
    wfc = np.ascontiguousarray(inputs["L0_attn_W_fc"])
    gam = np.ascontiguousarray(inputs["L0_attn_gamma"]).reshape(1, D)
    bet = np.ascontiguousarray(inputs["L0_attn_beta"]).reshape(1, D)
    wg = np.ascontiguousarray(inputs["L0_ffn_w_gate"])
    wu = np.ascontiguousarray(inputs["L0_ffn_w_up"])
    wd = np.ascontiguousarray(inputs["L0_ffn_w_down"])

    if "nc" not in _NC_CACHE:
        _NC_CACHE["nc"] = build_nc()
    nc = _NC_CACHE["nc"]

    in_maps = []
    for c in range(8):
        b = c // 4
        s0 = (c % 4) * 2048
        in_maps.append({
            "x": np.ascontiguousarray(x[b, s0:s0 + 2048]),
            "wq": wq, "wk": wk, "wv": wv, "wfc": wfc,
            "gamma": gam, "beta": bet,
            "wg": wg, "wu": wu, "wd": wd,
        })
    res = run_bass_kernel_spmd(nc, in_maps, core_ids=list(range(8)))
    out = np.empty((2, 8192, 1024), dtype=np.float32)
    for c in range(8):
        b = c // 4
        s0 = (c % 4) * 2048
        out[b, s0:s0 + 2048] = np.asarray(res.results[c]["y"])
    return out
